# version 14
# speedup vs baseline: 1.7091x; 1.0472x over previous
import contextlib
import numpy as np
import concourse.bass as bass
import concourse.mybir as mybir
from concourse.bass_utils import run_bass_kernel_spmd

F32 = mybir.dt.float32
BF16 = mybir.dt.bfloat16
I32 = mybir.dt.int32
AF = mybir.ActivationFunctionType
ALU = mybir.AluOpType

D = 1024
T = 2048
KT = 8
TP1 = T + 1
NLAYER = 6
HEADS, DH, PAIRS, WIN, NW = 16, 64, 8, 256, 8
FI = 2730
FIP = 2816
FTI = 22
DI = 2048
DIT = 16
DST = 16
DTR = 64
DFF = 4096
CH = 512
NCH = 4
PI = 3.14159265358979323846



def _lhsT_tiles(w_in_out, kt, mt):
    din, dout = w_in_out.shape
    assert din == 128 * kt and dout == 128 * mt
    a = w_in_out.reshape(kt, 128, mt, 128)
    return np.ascontiguousarray(a.transpose(1, 0, 2, 3))


def _pad_rows(w, rows):
    out = np.zeros((rows, w.shape[1]), dtype=w.dtype)
    out[: w.shape[0]] = w
    return out


def _pad_cols(w, cols):
    out = np.zeros((w.shape[0], cols), dtype=w.dtype)
    out[:, : w.shape[1]] = w
    return out


def _percol(v, nt):
    return np.ascontiguousarray(np.asarray(v).reshape(nt, 128).T)


def host_prepare(inputs):
    f32 = lambda x: np.ascontiguousarray(np.asarray(x), dtype=np.float32)
    H = {}
    Wqkv = f32(inputs["L0_ffn_Wqkv"])
    a = Wqkv.reshape(NLAYER, KT, 128, 3, PAIRS, 128)
    H["w_qkv"] = np.ascontiguousarray(a.transpose(0, 3, 4, 2, 1, 5))
    Wo = f32(inputs["L0_ffn_Wo"])
    H["w_wo"] = np.stack([_lhsT_tiles(Wo[l], KT, KT) for l in range(NLAYER)])
    W1 = f32(inputs["L0_ffn_W1"])
    H["w_w1a"] = np.stack([_lhsT_tiles(_pad_cols(W1[l][:, :FI], FIP), KT, FTI)
                           for l in range(NLAYER)])
    H["w_w1g"] = np.stack([_lhsT_tiles(_pad_cols(W1[l][:, FI:], FIP), KT, FTI)
                           for l in range(NLAYER)])
    W2 = f32(inputs["L0_ffn_W2"])
    H["w_w2"] = np.stack([_lhsT_tiles(_pad_rows(W2[l], FIP), FTI, KT)
                          for l in range(NLAYER)])
    H["p_lng"] = np.stack([_percol(f32(inputs["L0_ffn_ln_g"])[l], KT) for l in range(NLAYER)])
    H["p_lnb"] = np.stack([_percol(f32(inputs["L0_ffn_ln_b"])[l], KT) for l in range(NLAYER)])
    b1 = f32(inputs["L0_ffn_b1"])
    H["p_b1a"] = np.stack([_percol(_pad_rows(b1[l][:FI, None], FIP)[:, 0], FTI)
                           for l in range(NLAYER)])
    H["p_b1g"] = np.stack([_percol(_pad_rows(b1[l][FI:, None], FIP)[:, 0], FTI)
                           for l in range(NLAYER)])
    H["p_b2"] = np.stack([_percol(f32(inputs["L0_ffn_b2"])[l], KT) for l in range(NLAYER)])
    qs = f32(inputs["L0_ffn_q_scale"])
    ks = f32(inputs["L0_ffn_k_scale"])
    qpat = np.zeros((NLAYER, 2, 128), np.float32)
    kpat = np.zeros((NLAYER, 2, 128), np.float32)
    for l in range(NLAYER):
        qpat[l, 0, 0:64] = qs[l]
        qpat[l, 1, 64:128] = qs[l]
        kpat[l, 0, 0:64] = ks[l]
        kpat[l, 1, 64:128] = ks[l]
    H["c_qpat"], H["c_kpat"] = qpat, kpat

    H["m_inproj"] = _lhsT_tiles(f32(inputs["L0_attn_in_proj_w"]).T.copy(), KT, 32)
    xp = f32(inputs["L0_attn_x_proj_w"]).T.copy()
    H["m_xproj"] = np.ascontiguousarray(xp.reshape(DIT, 128, 96).transpose(1, 0, 2))
    dtp = f32(inputs["L0_attn_dt_proj_w"]).T.copy()
    H["m_dtproj"] = np.ascontiguousarray(dtp.reshape(64, DIT, 128))
    H["m_outproj"] = _lhsT_tiles(f32(inputs["L0_attn_out_proj_w"]).T.copy(), DIT, KT)
    H["m_w1"] = _lhsT_tiles(f32(inputs["L0_attn_w1"]).T.copy(), KT, 32)
    H["m_w3"] = _lhsT_tiles(f32(inputs["L0_attn_w3"]).T.copy(), KT, 32)
    H["m_w2"] = _lhsT_tiles(f32(inputs["L0_attn_w2"]).T.copy(), 32, KT)
    cw = f32(inputs["L0_attn_conv_w"]).reshape(DI, 4)
    H["m_convw"] = np.ascontiguousarray(cw.reshape(DIT, 128, 4).transpose(1, 0, 2))
    H["m_convb"] = _percol(f32(inputs["L0_attn_conv_b"]), DIT)
    H["m_dtb"] = _percol(f32(inputs["L0_attn_dt_proj_b"]), DIT)
    al = f32(inputs["L0_attn_A_log"])
    H["m_alog"] = np.ascontiguousarray(al.reshape(DIT, 128, DST).transpose(1, 0, 2))
    H["m_D"] = _percol(f32(inputs["L0_attn_D"]), DIT)
    H["p_n1"] = _percol(f32(inputs["L0_attn_norm1_w"]), KT)
    H["p_n2"] = _percol(f32(inputs["L0_attn_norm2_w"]), KT)

    mk = np.zeros((4, 128, WIN), np.float32)
    r = np.arange(128)[:, None]
    io = np.arange(WIN)[None, :]
    for jc in range(4):
        jg = 128 * jc + r
        mk[jc] = ((io <= jg) & (io >= jg - 256)).astype(np.float32)
    H["c_mask"] = np.ascontiguousarray(mk)
    H["c_eye"] = np.eye(128, dtype=np.float32)
    rp = np.zeros((128, 128), np.float32)
    for d in range(128):
        base, o = (d // 64) * 64, d % 64
        if o < 32:
            rp[base + o + 32, d] = -1.0
        else:
            rp[base + o - 32, d] = 1.0
    H["c_rperm"] = rp
    pb = np.zeros((16, 16, 128), np.float32)
    for n in range(16):
        pb[n, n, :] = 1.0
    H["c_patb"] = np.ascontiguousarray(pb.transpose(1, 0, 2))
    H["c_ones2"] = np.concatenate([np.repeat([[1.0, 0.0]], 64, 0),
                                   np.repeat([[0.0, 1.0]], 64, 0)]).astype(np.float32)
    return H



ENGINES = ("tensor", "vector", "scalar", "gpsimd", "sync")


class Buf:
    __slots__ = ("name", "last_write", "readers")

    def __init__(self, name):
        self.name = name
        self.last_write = None
        self.readers = []


class Ev:
    __slots__ = ("eng", "idx", "dma_sem", "dma_val")

    def __init__(self, eng, idx, dma_sem=None, dma_val=0):
        self.eng, self.idx, self.dma_sem, self.dma_val = eng, idx, dma_sem, dma_val


class Prog:
    def __init__(self, nc, es):
        self.nc = nc
        self.ops = {e: [] for e in ENGINES}
        self.flagged = {e: [] for e in ENGINES}
        self.eng_sem = {e: es.enter_context(nc.semaphore(f"sem_{e}")) for e in ENGINES}
        self.dma_pools = {
            "gpsimd": [es.enter_context(nc.semaphore(f"dmag{i}")) for i in range(12)],
            "sync": [es.enter_context(nc.semaphore(f"dmas{i}")) for i in range(8)],
        }
        self.dma_counts = {e: [0] * len(p) for e, p in self.dma_pools.items()}
        self.dma_rr = {e: 0 for e in self.dma_pools}
        self.last_drain = {e: -1 for e in ENGINES}
        self.waited = {e: {} for e in ENGINES}
        self.waited_dma = {e: {} for e in ENGINES}

    def _dep(self, eng, waits, drains, ev):
        if ev is None:
            return
        if ev.dma_sem is not None:
            key = id(ev.dma_sem)
            if self.waited_dma[eng].get(key, -1) < ev.dma_val:
                waits.append((ev.dma_sem, ev.dma_val))
                self.waited_dma[eng][key] = ev.dma_val
        else:
            if self.waited[eng].get(ev.eng, -1) < ev.idx:
                self.flagged[ev.eng][ev.idx] = True
                waits.append((ev.eng, ev.idx))
                self.waited[eng][ev.eng] = ev.idx

    def op(self, eng, fn, reads=(), writes=(), deps=(), dma=False):
        waits, drains = [], [False]
        for b in reads:
            self._dep(eng, waits, drains, b.last_write)
        for b in writes:
            self._dep(eng, waits, drains, b.last_write)
            for r in b.readers:
                self._dep(eng, waits, drains, r)
        for ev in deps:
            self._dep(eng, waits, drains, ev)
        idx = len(self.ops[eng])
        if drains[0]:
            self.last_drain[eng] = idx - 1
        dma_sem = dma_val = None
        if dma:
            pool = self.dma_pools[eng]
            slot = self.dma_rr[eng] % len(pool)
            self.dma_rr[eng] += 1
            prev = self.dma_counts[eng][slot]
            if prev > 0:
                self._dep(eng, waits, drains, Ev(eng, -1, pool[slot], prev))
            self.dma_counts[eng][slot] = prev + 16
            dma_sem, dma_val = pool[slot], prev + 16
        self.ops[eng].append(dict(fn=fn, waits=waits, drain=drains[0],
                                  dma_sem=dma_sem, dma_val=dma_val))
        self.flagged[eng].append(False)
        ev = Ev(eng, idx, dma_sem, dma_val if dma else 0)
        for b in writes:
            b.last_write = ev
            b.readers = []
        for b in reads:
            b.readers.append(ev)
        return ev

    def need(self, ev):
        if ev is not None and ev.dma_sem is None:
            self.flagged[ev.eng][ev.idx] = True

    def barrier(self):
        evs = []
        for e in ENGINES:
            for i in range(len(self.ops[e]) - 1, -1, -1):
                if self.ops[e][i]["fn"] is not None and self.ops[e][i]["dma_sem"] is None:
                    evs.append(Ev(e, i))
                    break
        for e, pool in self.dma_pools.items():
            for i, s in enumerate(pool):
                if self.dma_counts[e][i] > 0:
                    evs.append(Ev(e, -1, s, self.dma_counts[e][i]))
        for e in ENGINES:
            w, d = [], [False]
            for ev in evs:
                if ev.dma_sem is not None or ev.eng != e:
                    self._dep(e, w, d, ev)
            if w:
                self.ops[e].append(dict(fn=None, waits=w, drain=False,
                                        dma_sem=None, dma_val=None))
                self.flagged[e].append(False)

    def finalize(self, final_waits):
        prefix = {}
        for e in ENGINES:
            c, pre = 0, []
            for f in self.flagged[e]:
                c += 1 if f else 0
                pre.append(c)
            prefix[e] = pre
        nc = self.nc
        with nc.Block() as block:
            for e in ENGINES:
                def make(e):
                    def body(h):
                        sem = self.eng_sem[e]
                        for i, rec in enumerate(self.ops[e]):
                            for w in rec["waits"]:
                                if isinstance(w[0], str):
                                    h.wait_ge(self.eng_sem[w[0]], prefix[w[0]][w[1]])
                                else:
                                    h.wait_ge(w[0], w[1])
                            if rec["drain"]:
                                h.drain()
                            if rec["fn"] is None:
                                continue
                            ins = rec["fn"](h)
                            if rec["dma_sem"] is not None:
                                ins.then_inc(rec["dma_sem"], 16)
                            elif self.flagged[e][i]:
                                ins.then_inc(sem, 1)
                        if e == "sync":
                            fw = {}
                            for ev in final_waits:
                                if ev.dma_sem is not None:
                                    k = id(ev.dma_sem)
                                    if k not in fw or fw[k][1] < ev.dma_val:
                                        fw[k] = (ev.dma_sem, ev.dma_val)
                            for s, v in fw.values():
                                h.wait_ge(s, v)
                    return body
                getattr(block, e)(make(e))


class _TapDone(Exception):
    pass



def build_program(tap=None):
    nc = bass.Bass("TRN2", target_bir_lowering=False)
    es = contextlib.ExitStack()
    P = Prog(nc, es)

    dr = {}
    def din(name, shape):
        dr[name] = nc.dram_tensor(name, list(shape), F32, kind="ExternalInput").ap()

    din("x", (128, KT, T))
    y = nc.dram_tensor("y", [KT, 128, T], F32, kind="ExternalOutput").ap()
    for nm, sh in [("w_qkv", (NLAYER, 3, PAIRS, 128, KT, 128)),
                   ("w_wo", (NLAYER, 128, KT, KT, 128)),
                   ("w_w1a", (NLAYER, 128, KT, FTI, 128)),
                   ("w_w1g", (NLAYER, 128, KT, FTI, 128)),
                   ("w_w2", (NLAYER, 128, FTI, KT, 128)),
                   ("p_lng", (NLAYER, 128, KT)), ("p_lnb", (NLAYER, 128, KT)),
                   ("p_b1a", (NLAYER, 128, FTI)), ("p_b1g", (NLAYER, 128, FTI)),
                   ("p_b2", (NLAYER, 128, KT)),
                   ("c_qpat", (NLAYER, 2, 128)), ("c_kpat", (NLAYER, 2, 128)),
                   ("c_ones2", (128, 2)), ("c_patb", (16, 16, 128)),
                   ("m_inproj", (128, KT, 32, 128)), ("m_xproj", (128, DIT, 96)),
                   ("m_dtproj", (64, DIT, 128)), ("m_outproj", (128, DIT, KT, 128)),
                   ("m_w1", (128, KT, 32, 128)), ("m_w3", (128, KT, 32, 128)),
                   ("m_w2", (128, 32, KT, 128)),
                   ("m_convw", (128, DIT, 4)), ("m_convb", (128, DIT)),
                   ("m_dtb", (128, DIT)), ("m_alog", (128, DIT, DST)),
                   ("m_D", (128, DIT)), ("p_n1", (128, KT)), ("p_n2", (128, KT)),
                   ("c_mask", (4, 128, 256)), ("c_eye", (128, 128)),
                   ("c_rperm", (128, 128))]:
        din(nm, sh)
    xo_spill = nc.dram_tensor("xo_spill", [128, KT, T], F32, kind="Internal").ap()

    t_xi = nc.alloc_sbuf_tensor("XI", [128, KT, TP1], F32)
    t_xoa = nc.alloc_sbuf_tensor("XOA", [128, KT * TP1], F32)
    t_ph = nc.alloc_sbuf_tensor("PH", [128, 11264], F32)
    t_tabs = nc.alloc_sbuf_tensor("TABS", [128, 4, T], BF16)
    t_mask = nc.alloc_sbuf_tensor("MASK", [128, 4, 256], BF16)
    t_eye = nc.alloc_sbuf_tensor("EYE", [128, 128], BF16)
    t_rperm = nc.alloc_sbuf_tensor("RPERM", [128, 128], BF16)
    t_small = nc.alloc_sbuf_tensor("SMALL", [128, 1024], F32)
    t_misc = nc.alloc_sbuf_tensor("MISC", [128, 1792], BF16)
    t_rowb = nc.alloc_sbuf_tensor("ROWB", [2, 1024], BF16)
    t_stats = nc.alloc_sbuf_tensor("STATS", [4, 512], F32)
    t_sc1 = nc.alloc_sbuf_tensor("SC1", [128, 512], F32)
    t_sc2 = nc.alloc_sbuf_tensor("SC2", [128, 512], F32)
    t_const = nc.alloc_sbuf_tensor("CONST", [128, 4], F32)
    psum = nc.alloc_psum_tensor("PS", [128, 4096], F32)

    B = lambda n: Buf(n)
    b_xi = B("XI"); b_xo = B("XO")
    b_tabs = B("TABS"); b_mask = B("MASK"); b_eye = B("EYE"); b_rp = B("RPERM")
    b_small = B("SMALL"); b_misc = B("MISC"); b_rowb = B("ROWB"); b_stats = B("STATS")
    b_sc1 = B("SC1"); b_sc2 = B("SC2")
    b_ps = [B(f"PS{i}") for i in range(8)]
    b_spill = B("SPILL")
    arena = [B("arena0")]

    def ps(bank, n=512, off=0, p0=0, p1=128):
        return psum[p0:p1, bank * 512 + off: bank * 512 + off + n]

    def psb(*banks):
        return [b_ps[b] for b in banks]

    O_LNG, O_LNB, O_B1A, O_B1G, O_B2 = 0, 48, 96, 228, 360
    O_N1, O_N2, O_CONVW, O_CONVB, O_DTB = 408, 416, 424, 488, 504
    O_NEGA, O_D = 520, 776

    def small(off, n):
        return t_small[:, off: off + n]

    def view(tensor, off_w, nwords, dtype, inner=None):
        ap = tensor[:, off_w: off_w + nwords]
        if dtype == BF16:
            ap = ap.bitcast(BF16)
        if inner is not None:
            ap = ap.rearrange("p (a b) -> p a b", b=inner)
        return ap

    xo = view(t_xoa, 0, KT * TP1, F32, inner=TP1)
    b_const = B("CONSTS")

    def DMA(dst, src, wb, rb=(), q="gpsimd"):
        return P.op(q, lambda h: h.dma_start(dst, src), reads=list(rb),
                    writes=list(wb), dma=True)

    def MM(out_ap, lhsT, rhs, start, stop, rb, wb, tp=None):
        return P.op("tensor", lambda h: h.matmul(out_ap, lhsT, rhs, start=start,
                                                 stop=stop, tile_position=tp),
                    reads=list(rb), writes=list(wb))

    def VO(fn, rb, wb, eng="vector"):
        return P.op(eng, fn, reads=list(rb), writes=list(wb))

    def new_arena(name):
        P.barrier()
        arena[0] = B(name)
        return arena[0]

    for ci, cval in enumerate([1e-5, 1e-6, 1e-24, 1.0]):
        P.op("vector", lambda h, ci=ci, cval=cval: h.memset(
            t_const[:, ci:ci + 1], float(cval)), writes=[b_const])
        nc.const_aps.register(F32, float(cval), t_const[:, ci:ci + 1])
    for nm, off, n in [("p_lng", O_LNG, 48), ("p_lnb", O_LNB, 48),
                       ("p_b1a", O_B1A, 132), ("p_b1g", O_B1G, 132),
                       ("p_b2", O_B2, 48), ("p_n1", O_N1, 8), ("p_n2", O_N2, 8),
                       ("m_convw", O_CONVW, 64), ("m_convb", O_CONVB, 16),
                       ("m_dtb", O_DTB, 16), ("m_alog", O_NEGA, 256),
                       ("m_D", O_D, 16)]:
        src = dr[nm]
        if src.shape[0] == NLAYER:
            k = src.shape[2]
            DMA(small(off, n).rearrange("p (l k) -> p l k", k=k),
                src.rearrange("l p k -> p l k"), [b_small], q="sync")
        elif len(src.shape) == 3:
            DMA(small(off, n), src.rearrange("p a b -> p (a b)"), [b_small], q="sync")
        else:
            DMA(small(off, n), src, [b_small], q="sync")
    DMA(t_mask[:], dr["c_mask"].rearrange("m p c -> p m c"), [b_mask])
    DMA(t_eye[:], dr["c_eye"], [b_eye])
    DMA(t_rperm[:], dr["c_rperm"], [b_rp])
    DMA(t_misc[0:2, 0:768].rearrange("p (l c) -> p l c", c=128),
        dr["c_qpat"].rearrange("l p c -> p l c"), [b_misc])
    DMA(t_misc[0:2, 768:1536].rearrange("p (l c) -> p l c", c=128),
        dr["c_kpat"].rearrange("l p c -> p l c"), [b_misc])
    DMA(t_misc[:, 1536:1538], dr["c_ones2"], [b_misc])
    VO(lambda h: h.memset(t_misc[:, 1538:1539], 1.0), [], [b_misc])
    VO(lambda h: h.memset(t_misc[:, 1540:1668], 1.0), [], [b_misc])

    def qpat_ap(l):
        return t_misc[0:2, l * 128:(l + 1) * 128]

    def kpat_ap(l):
        return t_misc[0:2, 768 + l * 128: 768 + (l + 1) * 128]

    ONESCOL = t_misc[:, 1538:1539]
    ONES2 = t_misc[:, 1536:1538]

    def onesrow(p0, n):
        return t_misc[p0:p0 + 1, 1540:1540 + n]

    VO(lambda h: h.memset(t_small[0:1, 800:928], 1.0), [], [b_small])
    VO(lambda h: h.activation(small(O_NEGA, 256), small(O_NEGA, 256), AF.Exp),
       [b_small], [b_small], eng="scalar")
    VO(lambda h: h.tensor_scalar_mul(small(O_NEGA, 256), small(O_NEGA, 256), -1.0),
       [b_small], [b_small])

    ph_f = view(t_ph, 0, 2048, F32)
    ph_a = view(t_ph, 2048, 2048, F32)
    ph_i = view(t_ph, 4096, 2048, F32)
    ph_sin = view(t_ph, 6144, 2048, F32)
    ph_cos = view(t_ph, 8192, 2048, F32)
    ph_col = view(t_ph, 10240, 8, F32)
    ba = arena[0]
    VO(lambda h: h.iota(ph_col[:, 0:1], pattern=[[0, 1]], base=0,
                        channel_multiplier=1, allow_small_or_imprecise_dtypes=True),
       [], [ba], eng="gpsimd")
    VO(lambda h: h.iota(ph_f, pattern=[[1, 2048]], base=0, channel_multiplier=0,
                        allow_small_or_imprecise_dtypes=True), [], [ba], eng="gpsimd")
    VO(lambda h: h.tensor_scalar_mul(ph_col[:, 1:2], ph_col[:, 0:1], 1.0 / 32.0), [ba], [ba])
    VO(lambda h: h.tensor_copy(ph_i.bitcast(I32)[:, 0:1], ph_col[:, 1:2]), [ba], [ba])
    VO(lambda h: h.tensor_copy(ph_col[:, 2:3], ph_i.bitcast(I32)[:, 0:1]), [ba], [ba])
    VO(lambda h: h.tensor_sub(ph_col[:, 1:2], ph_col[:, 1:2], ph_col[:, 2:3]), [ba], [ba])
    VO(lambda h: h.tensor_single_scalar(ph_col[:, 4:5], ph_col[:, 1:2], 0.0, ALU.is_lt),
       [ba], [ba])
    VO(lambda h: h.tensor_add(ph_col[:, 1:2], ph_col[:, 1:2], ph_col[:, 4:5]), [ba], [ba])
    VO(lambda h: h.activation(ph_col[:, 2:3], ph_col[:, 1:2], AF.Exp,
                              scale=-float(np.log(10000.0))), [ba], [ba], eng="scalar")
    VO(lambda h: h.tensor_scalar(ph_col[:, 3:4], ph_col[:, 1:2], 64.0 / 89.6,
                                 25.6 / 89.6, ALU.mult, ALU.add), [ba], [ba])
    VO(lambda h: h.activation(ph_col[:, 3:4], ph_col[:, 3:4], AF.Ln), [ba], [ba], eng="scalar")
    VO(lambda h: h.tensor_scalar(ph_a, ph_f, 1.0 / 128.0, -8.0, ALU.mult, ALU.add), [ba], [ba])
    VO(lambda h: h.tensor_scalar(ph_f, ph_f, ph_col[:, 2:3], None, ALU.mult), [ba], [ba])
    VO(lambda h: h.tensor_scalar_mul(ph_sin, ph_f, 1.0 / (2 * PI)), [ba], [ba])
    VO(lambda h: h.tensor_copy(ph_i.bitcast(I32), ph_sin), [ba], [ba])
    VO(lambda h: h.tensor_copy(ph_sin, ph_i.bitcast(I32)), [ba], [ba])
    VO(lambda h: h.scalar_tensor_tensor(ph_f, ph_sin, -2 * PI, ph_f, ALU.mult, ALU.add), [ba], [ba])
    for _ in range(2):
        VO(lambda h: h.tensor_single_scalar(ph_sin, ph_f, PI, ALU.is_gt), [ba], [ba])
        VO(lambda h: h.scalar_tensor_tensor(ph_f, ph_sin, -2 * PI, ph_f, ALU.mult, ALU.add),
           [ba], [ba])
        VO(lambda h: h.tensor_single_scalar(ph_sin, ph_f, -PI, ALU.is_lt), [ba], [ba])
        VO(lambda h: h.scalar_tensor_tensor(ph_f, ph_sin, 2 * PI, ph_f, ALU.mult, ALU.add),
           [ba], [ba])
    VO(lambda h: h.tensor_scalar(ph_f, ph_f, PI * 0.999999, -PI * 0.999999, ALU.min, ALU.max),
       [ba], [ba])
    VO(lambda h: h.activation(ph_sin, ph_f, AF.Sin), [ba], [ba], eng="scalar")
    VO(lambda h: h.activation(ph_cos, ph_f, AF.Sin, scale=0.5), [ba], [ba], eng="scalar")
    VO(lambda h: h.tensor_mul(ph_cos, ph_cos, ph_cos), [ba], [ba])
    VO(lambda h: h.tensor_scalar(ph_cos, ph_cos, -2.0, 1.0, ALU.mult, ALU.add), [ba], [ba])
    VO(lambda h: h.tensor_scalar(ph_a, ph_a, ph_col[:, 3:4], None, ALU.mult), [ba], [ba])
    VO(lambda h: h.activation(ph_f, ph_a, AF.Exp), [ba], [ba], eng="scalar")
    VO(lambda h: h.activation(ph_a, ph_a, AF.Exp, scale=-1.0), [ba], [ba], eng="scalar")
    for i, (trig, sc) in enumerate([(ph_cos, ph_f), (ph_sin, ph_f),
                                    (ph_cos, ph_a), (ph_sin, ph_a)]):
        VO(lambda h, i=i, trig=trig, sc=sc: h.tensor_mul(t_tabs[:, i, :], trig, sc),
           [ba], [b_tabs])

    if tap == "tab0":
        P.barrier()
        outs = [P.op("gpsimd", lambda h: h.dma_start(y[0, :, :], t_tabs[:, 0, :]),
                     reads=[b_tabs], dma=True),
                P.op("gpsimd", lambda h: h.dma_start(y[1, :, :], t_tabs[:, 1, :]),
                     reads=[b_tabs], dma=True),
                P.op("gpsimd", lambda h: h.dma_start(y[2, :, 0:8], ph_col[:, 0:8]),
                     reads=[arena[0]], dma=True)]
        P.finalize(outs)
        return nc
    for k in range(KT):
        DMA(xo[:, k, 1:TP1], dr["x"][:, k, :], [b_xo], q="sync")
    VO(lambda h: h.memset(xo[:, :, 0:1], 0.0), [], [b_xo])
    VO(lambda h: h.memset(t_xi[:, :, 0:1], 0.0), [], [b_xi])

    def tap_out(get_src, c0=1):
        outs = []
        for k in range(KT):
            outs.append(P.op("sync", lambda h, k=k: h.dma_start(
                y[k, :, :], get_src(k, c0)), reads=[b_xi, b_xo, arena[0]], dma=True))
        P.finalize(outs)
        raise _TapDone()

    def maybe_tap(name, src3d, c0=1):
        if tap == name:
            tap_out(lambda k, c0: src3d[:, k, c0:c0 + T], c0)

    def maybe_raw(name, ap, bufs):
        if tap == name:
            P.barrier()
            outs = [P.op("gpsimd", lambda h: h.dma_start(y[0, 0:ap.shape[0], :], ap),
                         reads=list(bufs), dma=True)]
            for k in range(1, KT):
                outs.append(P.op("sync", lambda h, k=k: h.dma_start(
                    y[k, :, :], xo[:, k, 1:TP1]), reads=[b_xo], dma=True))
            P.finalize(outs)
            raise _TapDone()

    def stream_ap(stream, k, tc, shift, n=512):
        c0 = 1 if (not shift or k < KT // 2) else 0
        return stream[:, k, c0 + tc * n: c0 + (tc + 1) * n]

    SC1B = t_sc1[:].bitcast(BF16)
    b_sc1s = [B("sc1a"), B("sc1b")]
    sc1_slot = [SC1B[:, 0:512], SC1B[:, 512:1024]]
    ONESF32 = t_const[:, 3:4]

    def norm_stats(stream, tc, shift, kind):
        sxb, sxxb = (6, 7) if tc % 2 == 0 else (2, 3)
        for k in range(KT):
            src = stream_ap(stream, k, tc, shift)
            sl = k % 2
            if kind == "ln":
                MM(ps(sxb, 512, p1=1), ONESF32, src,
                   k == 0, k == KT - 1, [b_xi, b_xo, b_const], psb(sxb))
            VO(lambda h, src=src, sl=sl: h.activation(sc1_slot[sl], src, AF.Square),
               [b_xi, b_xo], [b_sc1s[sl]], eng="scalar")
            MM(ps(sxxb, 512, p1=1), ONESCOL, sc1_slot[sl],
               k == 0, k == KT - 1, [b_sc1s[sl], b_misc], psb(sxxb))
        eps = 1e-5 if kind == "ln" else 1e-6
        if kind == "ln":
            VO(lambda h: h.tensor_scalar_mul(t_rowb[0:1, 0:512], ps(sxb, 512, p1=1),
                                             1.0 / D), psb(sxb), [b_rowb])
            VO(lambda h: h.tensor_mul(t_stats[0:1, :], t_rowb[0:1, 0:512],
                                      t_rowb[0:1, 0:512]), [b_rowb], [b_stats])
            VO(lambda h: h.tensor_scalar_mul(ps(sxxb, 512, p1=1), ps(sxxb, 512, p1=1),
                                             1.0 / D), psb(sxxb), psb(sxxb))
            VO(lambda h: h.tensor_sub(ps(sxxb, 512, p1=1), ps(sxxb, 512, p1=1),
                                      t_stats[0:1, :]), psb(sxxb) + [b_stats], psb(sxxb))
        else:
            VO(lambda h: h.tensor_scalar_mul(ps(sxxb, 512, p1=1), ps(sxxb, 512, p1=1),
                                             1.0 / D), psb(sxxb), psb(sxxb))
        VO(lambda h, eps=eps: h.activation(ps(sxxb, 512, p1=1), ps(sxxb, 512, p1=1),
                                           AF.Ln, bias=float(eps)),
           psb(sxxb), psb(sxxb), eng="scalar")
        VO(lambda h: h.activation(t_stats[0:1, :], ps(sxxb, 512, p1=1),
                                  AF.Exp, scale=-0.5), psb(sxxb), [b_stats], eng="scalar")

    def norm_apply(stream, dst_fn, shift, kind, g_ap=None, b_ap=None,
                   tcs=range(4), dst_bufs=()):
        for tc in tcs:
            norm_stats(stream, tc, shift, kind)
            mb, rb = (4, 5) if tc % 2 == 0 else (0, 1)
            if kind == "ln":
                MM(ps(mb, 512), onesrow(0, 128), t_rowb[0:1, 0:512], True, True,
                   [b_rowb, b_misc], psb(mb))
            MM(ps(rb, 512), t_small[0:1, 800:928], t_stats[0:1, :], True, True,
               [b_stats, b_small], psb(rb))
            for k in range(KT):
                src = stream_ap(stream, k, tc, shift)
                dst = dst_fn(k, tc)
                if kind == "rms":
                    VO(lambda h, src=src, dst=dst, k=k, rb=rb: h.scalar_tensor_tensor(
                        dst, src, g_ap[:, k: k + 1], ps(rb, 512), ALU.mult, ALU.mult),
                       [b_xi, b_xo, b_small] + psb(rb), list(dst_bufs))
                else:
                    VO(lambda h, src=src, mb=mb: h.tensor_sub(t_sc2[:], src,
                                                              ps(mb, 512)),
                       [b_xi, b_xo] + psb(mb), [b_sc2])
                    if g_ap is None:
                        VO(lambda h, dst=dst, rb=rb: h.tensor_mul(dst, t_sc2[:],
                                                                  ps(rb, 512)),
                           [b_sc2] + psb(rb), list(dst_bufs))
                    else:
                        VO(lambda h, dst=dst, k=k, rb=rb: h.scalar_tensor_tensor(
                            dst, t_sc2[:], g_ap[:, k: k + 1], ps(rb, 512),
                            ALU.mult, ALU.mult), [b_sc2, b_small] + psb(rb),
                           list(dst_bufs))
                        VO(lambda h, dst=dst, k=k: h.tensor_scalar_add(
                            dst, dst, b_ap[:, k: k + 1]),
                           [b_small] + list(dst_bufs), list(dst_bufs))

    def spill_xo():
        evs = []
        for k in range(KT):
            evs.append(DMA(xo_spill[:, k, :], xo[:, k, 1:TP1], [b_spill], [b_xo], q="sync"))
        return evs

    def reload_xo(coef):
        nonlocal_barrier = new_arena("xo_back")
        for k in range(KT):
            DMA(xo[:, k, 1:TP1], xo_spill[:, k, :], [b_xo], [b_spill], q="sync")
        VO(lambda h: h.memset(xo[:, :, 0:1], 0.0), [], [b_xo])
        for k in range(KT):
            for tc in range(4):
                dst = xo[:, k, 1 + tc * 512: 1 + (tc + 1) * 512]
                src = t_xi[:, k, 1 + tc * 512: 1 + (tc + 1) * 512]
                VO(lambda h, dst=dst, src=src: h.scalar_tensor_tensor(
                    dst, src, float(coef), dst, ALU.mult, ALU.add),
                   [b_xi, b_xo], [b_xo])

    def attn_sublayer(l):
        xsh = view(t_xoa, 0, 8192, BF16, inner=T)
        ao = view(t_xoa, 8192, 8192, BF16, inner=T)
        wqk_s = [view(t_ph, 0, 1024, BF16, inner=128),
                 view(t_ph, 1024, 1024, BF16, inner=128)]
        wvu_s = [view(t_ph, 2048, 1024, BF16, inner=256),
                 view(t_ph, 3072, 1024, BF16, inner=256)]
        qsb = view(t_ph, 4096, 1024, BF16)
        ksb = view(t_ph, 5120, 1024, BF16)
        vtm = view(t_ph, 6144, 2080, BF16, inner=260)
        ptb = view(t_ph, 8224, 1024, BF16, inner=512)
        rsrow = view(t_ph, 8224, 1024, BF16)
        qt2 = view(t_ph, 9248, 1024, BF16)
        drw = view(t_ph, 10272, 512, BF16)
        wo_slot = [view(t_ph, 6144, 512, BF16, inner=128),
                   view(t_ph, 6656, 512, BF16, inner=128)]
        ba = new_arena(f"attn{l}")
        b_wqk = [B("wqk0"), B("wqk1")]; b_wvu = [B("wvu0"), B("wvu1")]
        b_wo = [B("wo0"), B("wo1")]
        b_q = B("q"); b_k = B("k"); b_v = B("v")
        b_pt = [B("pt0"), B("pt1"), B("pt2"), B("pt3")]
        b_sh = [B(f"xsh{i}") for i in range(4)]
        b_ao = B("ao"); b_qt = B("qt"); b_dr = B("drw")
        VO(lambda h: h.memset(vtm[:, :, 64:65], 1.0), [], [b_v])
        VO(lambda h: h.memset(vtm[:, :, 194:195], 1.0), [], [b_v])
        for tc in range(4):
            for k in range(KT):
                src = stream_ap(t_xi, k, tc, True)
                VO(lambda h, src=src, k=k, tc=tc: h.tensor_copy(
                    xsh[:, k, tc * 512:(tc + 1) * 512], src), [b_xi], [b_sh[tc]])

        def evac_av(hp, half):
            VO(lambda h: h.activation(psum[64:65, 0:1024], psum[64:65, 0:1024], AF.Ln),
               psb(0, 1), psb(0, 1), eng="scalar")
            VO(lambda h: h.activation(drw[64:65, 0:1024], psum[64:65, 0:1024],
                                      AF.Exp, scale=-1.0), psb(0, 1), [b_dr], eng="scalar")
            VO(lambda h: h.activation(psum[32:33, 2048:3072], psum[32:33, 2048:3072],
                                      AF.Ln), psb(4, 5), psb(4, 5), eng="scalar")
            VO(lambda h: h.activation(drw[32:33, 0:1024], psum[32:33, 2048:3072],
                                      AF.Exp, scale=-1.0), psb(4, 5), [b_dr], eng="scalar")
            for cc in range(2):
                MM(psum[0:64, 3072 + cc * 512: 3072 + (cc + 1) * 512],
                   onesrow(64, 64), drw[64:65, cc * 512:(cc + 1) * 512], True, True,
                   [b_dr, b_misc], psb(6, 7), tp=(64, 0))
                MM(psum[64:128, 3072 + cc * 512: 3072 + (cc + 1) * 512],
                   onesrow(32, 64), drw[32:33, cc * 512:(cc + 1) * 512], True, True,
                   [b_dr, b_misc], psb(6, 7), tp=(32, 64))
            VO(lambda h: h.tensor_copy(qt2[:, 0:1024], psum[:, 3072:4096]),
               psb(6, 7), [b_qt])
            for cc in range(2):
                c0 = half * 1024 + cc * 512
                VO(lambda h, cc=cc, c0=c0: h.tensor_mul(
                    ao[0:64, hp, c0:c0 + 512],
                    psum[0:64, cc * 512:(cc + 1) * 512],
                    qt2[0:64, cc * 512:(cc + 1) * 512]), psb(0, 1) + [b_qt], [b_ao])
                VO(lambda h, cc=cc, c0=c0: h.tensor_mul(
                    ao[64:128, hp, c0:c0 + 512],
                    psum[64:128, 1024 + cc * 512: 1024 + (cc + 1) * 512],
                    qt2[64:128, cc * 512:(cc + 1) * 512]), psb(2, 3) + [b_qt], [b_ao])

        for un in range(4):
            usl = un % 2
            wvu = wvu_s[usl]
            DMA(wvu.rearrange("p k (w c) -> p k w c", c=128),
                dr["w_qkv"][l, 2, 2 * un: 2 * un + 2].rearrange("w p k c -> p k w c"),
                [b_wvu[usl]])
            for tt0 in range(0, 16, 2):
                for k in range(KT):
                    for tt in (tt0, tt0 + 1):
                        MM(ps(tt % 2, 256), xsh[:, k, tt * 128:(tt + 1) * 128],
                           wvu[:, k, :], k == 0, k == KT - 1,
                           [b_wvu[usl], b_sh[tt // 4]], psb(tt % 2))
                for tt in (tt0, tt0 + 1):
                    bk = tt % 2
                    for pp in range(2):
                        base = pp * 130
                        dst = vtm[:, tt, base:base + 130].rearrange(
                            "p (a b) -> p a b", b=65)[:, :, 0:64]
                        srcv = ps(bk, 128, off=pp * 128).rearrange(
                            "p (a b) -> p a b", b=64)
                        VO(lambda h, dst=dst, srcv=srcv: h.tensor_copy(dst, srcv),
                           psb(bk), [b_v])
            for hp in (2 * un, 2 * un + 1):
                psl = hp % 2
                vbase = (hp % 2) * 130
                wqk = wqk_s[psl]
                DMA(wqk.rearrange("p (w k) c -> p w k c", w=2),
                    dr["w_qkv"][l, 0:2, hp].rearrange("w p k c -> p w k c"), [b_wqk[psl]])
                for side in range(2):
                    wslab = wqk[:, side * 8:(side + 1) * 8, :]
                    sb = qsb if side == 0 else ksb
                    bsd = b_q if side == 0 else b_k
                    pat = qpat_ap(l) if side == 0 else kpat_ap(l)
                    tabc, tabs_ = (0, 1) if side == 0 else (2, 3)
                    for k in range(KT):
                        for tc in range(4):
                            MM(ps(tc, 512), wslab[:, k, :],
                               xsh[:, k, tc * 512:(tc + 1) * 512], k == 0, k == KT - 1,
                               [b_wqk[psl], b_sh[tc]], psb(tc))
                    for tc in range(4):
                        VO(lambda h, sb=sb, tc=tc: h.tensor_copy(
                            sb[:, tc * 512:(tc + 1) * 512], ps(tc, 512)), psb(tc), [bsd])
                    if hp == 0 and side == 0:
                        maybe_raw("q0raw", sb[:, :], [bsd])
                    VO(lambda h, sb=sb: h.tensor_mul(qt2[:], sb[:], sb[:]),
                       [bsd], [b_qt])
                    for n4 in range(4):
                        MM(psum[0:2, 2048 + n4 * 512: 2048 + (n4 + 1) * 512],
                           ONES2, qt2[:, n4 * 512:(n4 + 1) * 512], True, True,
                           [b_qt, b_misc], psb(4 + n4))
                    VO(lambda h: h.activation(psum[0:2, 2048:4096],
                                              psum[0:2, 2048:4096],
                                              AF.Ln, bias=1e-24), psb(4, 5, 6, 7),
                       psb(4, 5, 6, 7), eng="scalar")
                    VO(lambda h: h.activation(rsrow[0:2, :], psum[0:2, 2048:4096],
                                              AF.Exp, scale=-0.5),
                       psb(4, 5, 6, 7), list(b_pt), eng="scalar")
                    if hp == 0 and side == 0:
                        maybe_raw("q0rs", rsrow[0:2, :], [b_pt[0]])
                    VO(lambda h, sb=sb, t=tabs_: h.tensor_mul(qt2[:], sb[:],
                                                              t_tabs[:, t, :]),
                       [bsd, b_tabs], [b_qt])
                    for tc in range(4):
                        MM(ps(4 + tc, 512), t_rperm[:], qt2[:, tc * 512:(tc + 1) * 512],
                           True, True, [b_qt, b_rp], psb(4 + tc))
                    VO(lambda h, sb=sb, t=tabc: h.tensor_mul(sb[:], sb[:],
                                                             t_tabs[:, t, :]),
                       [bsd, b_tabs], [bsd])
                    for tc in range(4):
                        VO(lambda h, sb=sb, tc=tc: h.tensor_add(
                            sb[:, tc * 512:(tc + 1) * 512],
                            sb[:, tc * 512:(tc + 1) * 512],
                            ps(4 + tc, 512)), [bsd] + psb(4 + tc), [bsd])
                    for tc in range(4):
                        MM(ps(4 + tc, 512), pat, rsrow[0:2, tc * 512:(tc + 1) * 512],
                           True, True, list(b_pt) + [b_misc], psb(4 + tc))
                        VO(lambda h, sb=sb, tc=tc: h.tensor_mul(
                            sb[:, tc * 512:(tc + 1) * 512],
                            sb[:, tc * 512:(tc + 1) * 512],
                            ps(4 + tc, 512)), [bsd] + psb(4 + tc), [bsd])
                if hp == 0:
                    maybe_raw("q0", qsb[:, :], [b_q])
                    maybe_raw("k0", ksb[:, :], [b_k])
                for half in range(2):
                    pending = []

                    def emit_av(grp_list):
                        for (jc_, w_, st2, sp2) in grp_list:
                            jt_ = 2 * w_ + jc_ - 2
                            wc_ = (w_ % 4) * 256
                            MM(psum[0:65, wc_: wc_ + 256],
                               vtm[:, jt_, vbase:vbase + 65],
                               ptb[:, jc_, 0:256], st2, sp2,
                               [b_v, b_pt[jc_]], psb(0, 1))
                            MM(psum[64:128, 1024 + wc_: 1024 + wc_ + 256],
                               vtm[:, jt_, vbase + 65:vbase + 129],
                               ptb[:, jc_, 256:512], st2, sp2,
                               [b_v, b_pt[jc_]], psb(2, 3), tp=(0, 64))
                            MM(psum[32:33, 2048 + wc_: 2048 + wc_ + 256],
                               vtm[:, jt_, vbase + 64:vbase + 65],
                               ptb[:, jc_, 256:512], st2, sp2,
                               [b_v, b_pt[jc_]], psb(4, 5), tp=(0, 32))

                    for w in range(half * 4, half * 4 + 4):
                        jcs = [2, 3] if w == 0 else [0, 1, 2, 3]
                        groups = [jcs] if w == 0 else [[0, 1], [2, 3]]
                        for grp in groups:
                            for jc in grp:
                                jt = 2 * w + jc - 2
                                sbk = 6 + (jc % 2)
                                for hd in range(2):
                                    MM(ps(sbk, 256, off=hd * 256),
                                       ksb[hd * 64:(hd + 1) * 64,
                                           jt * 128:(jt + 1) * 128],
                                       qsb[hd * 64:(hd + 1) * 64,
                                           w * 256:(w + 1) * 256],
                                       True, True, [b_q, b_k], psb(sbk),
                                       tp=(hd * 64, 0))
                            j0 = grp[0]
                            VO(lambda h, j0=j0: h.activation(
                                view(t_ph, 8224 + j0 * 256, 512, BF16),
                                psum[:, 3072:4096], AF.Exp, scale=8.0),
                               psb(6, 7), [b_pt[j0], b_pt[j0 + 1]], eng="scalar")
                            VO(lambda h, j0=j0: h.tensor_mul(
                                view(t_ph, 8224 + j0 * 256, 512, BF16).rearrange(
                                    "p (j g c) -> p j g c", j=2, g=2),
                                view(t_ph, 8224 + j0 * 256, 512, BF16).rearrange(
                                    "p (j g c) -> p j g c", j=2, g=2),
                                t_mask[:, j0:j0 + 2, :].unsqueeze(2)
                                .broadcast_to([128, 2, 2, 256])),
                               [b_mask, b_pt[j0], b_pt[j0 + 1]],
                               [b_pt[j0], b_pt[j0 + 1]])
                            prev, pending = pending, [
                                (jc, w, jc == jcs[0], jc == jcs[-1]) for jc in grp]
                            emit_av(prev)
                    emit_av(pending)
                    pending = []
                    evac_av(hp, half)
                if hp == 0:
                    maybe_raw("ao0", ao[:, 0, :], [b_ao])
        for m in range(KT):
            wo = wo_slot[m % 2]
            DMA(wo, dr["w_wo"][l, :, :, m], [b_wo[m % 2], b_v])
            bko = 4 * (m % 2)
            for k in range(KT):
                for tc in range(4):
                    MM(ps(bko + tc, 512), wo[:, k, :], ao[:, k, tc * 512:(tc + 1) * 512],
                       k == 0, k == KT - 1, [b_wo[m % 2], b_ao], psb(bko + tc))
            for tc in range(4):
                dst = t_xi[:, m, 1 + tc * 512: 1 + (tc + 1) * 512]
                VO(lambda h, dst=dst, bk=bko + tc: h.tensor_add(dst, dst, ps(bk, 512)),
                   [b_xi] + psb(bko + tc), [b_xi])

    def ffn_sublayer(l):
        xn = view(t_xoa, 0, 8192, BF16, inner=T)
        w1a_s = [view(t_xoa, 8192, 1024, BF16, inner=256),
                 view(t_xoa, 11648, 1024, BF16, inner=256)]
        w1g_s = [view(t_xoa, 9216, 1024, BF16, inner=256),
                 view(t_xoa, 12672, 1024, BF16, inner=256)]
        w2c_s = [view(t_xoa, 10240, 1408, BF16, inner=128),
                 view(t_xoa, 13696, 1408, BF16, inner=128)]
        G = view(t_ph, 0, 11264, BF16, inner=1024)
        ba = new_arena(f"ffn{l}")
        b_xn = B("xn"); b_g = B("G")
        b_w1 = [B("w1a0"), B("w1a1")]; b_w2 = [B("w2a0"), B("w2a1")]
        norm_apply(t_xi, lambda k, tc: xn[:, k, tc * 512:(tc + 1) * 512], True, "ln",
                   g_ap=small(O_LNG + l * KT, KT), b_ap=small(O_LNB + l * KT, KT),
                   dst_bufs=[b_xn])
        for th in range(2):
            for blk in range(11):
                sl = blk % 2
                w1a, w1g = w1a_s[sl], w1g_s[sl]
                DMA(w1a.rearrange("p k (f c) -> p k f c", c=128),
                    dr["w_w1a"][l, :, :, 2 * blk: 2 * blk + 2, :], [b_w1[sl]])
                DMA(w1g.rearrange("p k (f c) -> p k f c", c=128),
                    dr["w_w1g"][l, :, :, 2 * blk: 2 * blk + 2, :], [b_w1[sl]])
                for ft in range(2):
                    fg = 2 * blk + ft
                    fb = 4 * (ft % 2)
                    for k in range(KT):
                        for tq in range(2):
                            tcg = th * 2 + tq
                            MM(ps(fb + tq, 512), w1a[:, k, ft * 128:(ft + 1) * 128],
                               xn[:, k, tcg * 512:(tcg + 1) * 512], k == 0, k == KT - 1,
                               [b_w1[sl], b_xn], psb(fb + tq))
                            MM(ps(fb + 2 + tq, 512), w1g[:, k, ft * 128:(ft + 1) * 128],
                               xn[:, k, tcg * 512:(tcg + 1) * 512], k == 0, k == KT - 1,
                               [b_w1[sl], b_xn], psb(fb + 2 + tq))
                    for tq in range(2):
                        tcg = th * 2 + tq
                        ssl = tq
                        VO(lambda h, fg=fg, tq=tq, fb=fb, ssl=ssl: h.activation(
                            sc1_slot[ssl], ps(fb + 2 + tq, 512), AF.Gelu,
                            bias=t_small[:, O_B1G + l * FTI + fg:
                                         O_B1G + l * FTI + fg + 1]),
                           psb(fb + 2 + tq), [b_sc1s[ssl]], eng="scalar")
                        VO(lambda h, fg=fg, tq=tq, fb=fb, ssl=ssl: h.scalar_tensor_tensor(
                            G[:, fg, tq * 512:(tq + 1) * 512], ps(fb + tq, 512),
                            t_small[:, O_B1A + l * FTI + fg: O_B1A + l * FTI + fg + 1],
                            sc1_slot[ssl], ALU.add, ALU.mult),
                           psb(fb + tq) + [b_sc1s[ssl], b_small], [b_g])
            for m in range(KT):
                w2c = w2c_s[m % 2]
                DMA(w2c, dr["w_w2"][l, :, :, m, :], [b_w2[m % 2]])
                bks = [4 + (m * 2) % 4, 6 + (m * 2) % 4 - 4] if False else \
                      [4 + 2 * (m % 2), 5 + 2 * (m % 2)]
                for kf in range(FTI):
                    for tq in range(2):
                        MM(ps(bks[tq], 512), w2c[:, kf, :],
                           G[:, kf, tq * 512:(tq + 1) * 512],
                           kf == 0, kf == FTI - 1, [b_w2[m % 2], b_g], psb(bks[tq]))
                for tq in range(2):
                    tcg = th * 2 + tq
                    dst = t_xi[:, m, 1 + tcg * 512: 1 + (tcg + 1) * 512]
                    VO(lambda h, dst=dst, bk=bks[tq], m=m: h.scalar_tensor_tensor(
                        dst, ps(bk, 512),
                        t_small[:, O_B2 + l * KT + m: O_B2 + l * KT + m + 1],
                        dst, ALU.add, ALU.add), psb(bks[tq]) + [b_small, b_xi], [b_xi])

    def mamba_sublayer():
        xinp = view(t_xoa, 0, 4096, BF16, inner=512)
        zsil = view(t_xoa, 4096, 4096, BF16, inner=512)
        dly2 = view(t_xoa, 8192, 4096, BF16, inner=512)
        wip_slots = [view(t_xoa, 12288, 2048, BF16, inner=128),
                     view(t_ph, 8984, 2048, BF16, inner=128)]
        xn1 = view(t_xoa, 14336, 2048, BF16, inner=512)
        patb = view(t_ph, 0, 1024, BF16, inner=128)
        xpj = view(t_ph, 1024, 768, BF16, inner=96)
        dtj = view(t_ph, 1792, 1024, BF16, inner=128)
        dtsb = view(t_ph, 2816, 256, BF16)
        bcsb = view(t_ph, 3072, 512, BF16)
        dA_s = [view(t_ph, 3584, 512, F32), view(t_ph, 4096, 512, F32)]
        hh_s = [view(t_ph, 4608, 512, F32), view(t_ph, 5120, 512, F32)]
        hc_s = [view(t_ph, 5632, 256, BF16), view(t_ph, 5888, 256, BF16)]
        dx_s = [view(t_ph, 6144, 256, BF16), view(t_ph, 6400, 256, BF16)]
        carry = view(t_ph, 6656, 256, F32)
        halo = view(t_ph, 6912, 24, BF16, inner=3)
        cvp = t_sc2[:].bitcast(BF16)[:, 0:512]
        wop_s = [view(t_ph, 6936, 1024, BF16, inner=128),
                 view(t_ph, 7960, 1024, BF16, inner=128)]
        ba = new_arena("mamba")
        b_xn1 = B("xn1"); b_xp = B("xinp"); b_z = B("z")
        b_dy = [B("dly2a"), B("dly2b")]
        b_wip = [B("wip0"), B("wip1")]; b_wop = [B("wop0"), B("wop1")]
        b_cst = B("mcst")
        b_dA = [B("dA0"), B("dA1")]; b_h = [B("h0"), B("h1")]
        b_hc = [B("hc0"), B("hc1")]; b_dx = [B("dx0"), B("dx1")]
        b_scan = B("scan")
        b_halo = B("halo"); b_carry = B("carry")
        DMA(patb[0:16, :, :].rearrange("p a b -> p (a b)"),
            dr["c_patb"].rearrange("p a b -> p (a b)"), [b_cst])
        DMA(xpj[:].rearrange("p a b -> p (a b)"),
            dr["m_xproj"].rearrange("p a b -> p (a b)"), [b_cst])
        DMA(dtj[0:64, :, :].rearrange("p a b -> p (a b)"),
            dr["m_dtproj"].rearrange("p a b -> p (a b)"), [b_cst])
        VO(lambda h: h.memset(carry[:], 0.0), [], [b_carry])
        VO(lambda h: h.memset(halo[:].rearrange("p a b -> p (a b)"), 0.0), [], [b_halo])
        for c in range(NCH):
            norm_apply(t_xi, lambda k, tc: xn1[:, k, 0:512], False, "rms",
                       g_ap=small(O_N1, KT), tcs=[c], dst_bufs=[b_xn1])
            for mg in range(8):
                sl = mg % 2
                wip = wip_slots[sl]
                DMA(wip.rearrange("p (k m) c -> p k m c", m=4),
                    dr["m_inproj"][:, :, mg * 4:(mg + 1) * 4, :], [b_wip[sl]])
                for k in range(KT):
                    for mi in range(4):
                        MM(ps(4 + mi, 512), wip[:, k * 4 + mi, :], xn1[:, k, :],
                           k == 0, k == KT - 1, [b_wip[sl], b_xn1], psb(4 + mi))
                for mi in range(4):
                    m = mg * 4 + mi
                    bk = 4 + mi
                    if m < 16:
                        VO(lambda h, m=m, bk=bk: h.tensor_copy(xinp[:, m, :],
                                                               ps(bk, 512)),
                           psb(bk), [b_xp])
                    else:
                        VO(lambda h, m=m, bk=bk: h.activation(
                            zsil[:, m - 16, :], ps(bk, 512), AF.Silu),
                           psb(bk), [b_z], eng="scalar")
            for tt in range(DIT):
                VO(lambda h, tt=tt: h.tensor_copy(t_sc2[:].bitcast(BF16)[:, 512:515],
                                                  xinp[:, tt, 509:512]), [b_xp], [b_sc2])
                VO(lambda h, tt=tt: h.tensor_scalar(
                    cvp[:], xinp[:, tt, :], t_small[:, O_CONVW + tt * 4 + 3:
                                                    O_CONVW + tt * 4 + 4], None,
                    ALU.mult), [b_xp, b_small], [b_scan])
                for kk in range(3):
                    sh = 3 - kk
                    VO(lambda h, tt=tt, kk=kk, sh=sh: h.scalar_tensor_tensor(
                        cvp[:, sh:512], xinp[:, tt, 0:512 - sh],
                        t_small[:, O_CONVW + tt * 4 + kk: O_CONVW + tt * 4 + kk + 1],
                        cvp[:, sh:512], ALU.mult, ALU.add),
                       [b_xp, b_small, b_scan], [b_scan])
                    VO(lambda h, tt=tt, kk=kk, sh=sh: h.scalar_tensor_tensor(
                        cvp[:, 0:sh], halo[:, tt, kk:3],
                        t_small[:, O_CONVW + tt * 4 + kk: O_CONVW + tt * 4 + kk + 1],
                        cvp[:, 0:sh], ALU.mult, ALU.add),
                       [b_halo, b_small, b_scan], [b_scan])
                VO(lambda h, tt=tt: h.activation(
                    xinp[:, tt, :], cvp[:], AF.Silu,
                    bias=t_small[:, O_CONVB + tt: O_CONVB + tt + 1]),
                   [b_scan, b_small], [b_xp], eng="scalar")
                VO(lambda h, tt=tt: h.tensor_copy(halo[:, tt, :],
                                                  t_sc2[:].bitcast(BF16)[:, 512:515]),
                   [b_sc2], [b_halo])
            for k in range(DIT):
                MM(ps(2, 512, p1=64), xpj[:, k, 0:64], xinp[:, k, :],
                   k == 0, k == DIT - 1, [b_cst, b_xp], psb(2))
                MM(ps(3, 512, p1=16), xpj[:, k, 64:80], xinp[:, k, :],
                   k == 0, k == DIT - 1, [b_cst, b_xp], psb(3))
                MM(ps(4, 512, p1=16), xpj[:, k, 80:96], xinp[:, k, :],
                   k == 0, k == DIT - 1, [b_cst, b_xp], psb(4))
            VO(lambda h: h.tensor_copy(dtsb[0:64, :], ps(2, 512, p1=64)), psb(2), [b_scan])
            VO(lambda h: h.tensor_copy(bcsb[0:16, 0:512], ps(3, 512, p1=16)), psb(3), [b_scan])
            VO(lambda h: h.tensor_copy(bcsb[0:16, 512:1024], ps(4, 512, p1=16)), psb(4), [b_scan])
            for tt in range(DIT):
                bk = 2 + tt % 2
                MM(ps(bk, 512), dtj[0:64, tt, :], dtsb[0:64, :], True, True,
                   [b_cst, b_scan], psb(bk))
                VO(lambda h, tt=tt, bk=bk: h.activation(
                    t_sc2[:], ps(bk, 512), AF.Exp,
                    bias=t_small[:, O_DTB + tt: O_DTB + tt + 1]),
                   psb(bk) + [b_small], [b_sc2], eng="scalar")
                VO(lambda h, tt=tt: h.activation(dly2[:, tt, :], t_sc2[:], AF.Ln,
                                                 bias=1.0),
                   [b_sc2], [b_dy[tt % 2]], eng="scalar")
            for tt in range(DIT):
                tsl = tt % 2
                VO(lambda h, tt=tt, tsl=tsl: h.tensor_mul(
                    dx_s[tsl][:], dly2[:, tt, :], xinp[:, tt, :]),
                   [b_dy[tsl], b_xp], [b_dx[tsl]])
                ybk = tt % 2
                def bcast(n):
                    nsl = n % 2
                    MM(ps(4 + nsl, 512), patb[0:16, n, :], bcsb[0:16, 0:512], True, True,
                       [b_cst, b_scan], psb(4 + nsl))
                    MM(ps(2 + nsl, 512), patb[0:16, n, :], bcsb[0:16, 512:1024],
                       True, True, [b_cst, b_scan], psb(2 + nsl))
                if tt == 0:
                    bcast(0)
                for n in range(DST):
                    nsl = n % 2
                    bb, cb, ub = 4 + nsl, 2 + nsl, 6 + nsl
                    dA, hh, hc = dA_s[nsl], hh_s[nsl], hc_s[nsl]
                    VO(lambda h, tt=tt, n=n, dA=dA: h.activation(
                        dA[:], dly2[:, tt, :], AF.Exp,
                        scale=t_small[:, O_NEGA + tt * 16 + n: O_NEGA + tt * 16 + n + 1]),
                       [b_dy[tsl], b_small], [b_dA[nsl]], eng="scalar")
                    VO(lambda h, tsl=tsl, bb=bb, ub=ub: h.tensor_mul(
                        ps(ub, 512), dx_s[tsl][:], ps(bb, 512)),
                       [b_dx[tsl]] + psb(bb), psb(ub))
                    cidx = tt * 16 + n
                    VO(lambda h, cidx=cidx, dA=dA, hh=hh, ub=ub: h.tensor_tensor_scan(
                        hh[:], dA[:], ps(ub, 512), carry[:, cidx:cidx + 1],
                        ALU.mult, ALU.add),
                       [b_dA[nsl], b_carry] + psb(ub), [b_h[nsl]])
                    VO(lambda h, cidx=cidx, hh=hh: h.tensor_copy(
                        carry[:, cidx:cidx + 1], hh[:, 511:512]),
                       [b_h[nsl]], [b_carry])
                    VO(lambda h, hh=hh, hc=hc, cb=cb: h.tensor_mul(
                        hc[:], hh[:], ps(cb, 512)), [b_h[nsl]] + psb(cb), [b_hc[nsl]])
                    if n < DST - 1:
                        bcast(n + 1)
                    elif tt < DIT - 1:
                        bcast(0)
                    MM(ps(ybk, 512), t_eye[:], hc[:], n == 0, n == DST - 1,
                       [b_eye, b_hc[nsl]], psb(ybk))
                VO(lambda h, tt=tt, ybk=ybk: h.scalar_tensor_tensor(
                    dly2[:, tt, :], xinp[:, tt, :],
                    t_small[:, O_D + tt: O_D + tt + 1], ps(ybk, 512),
                    ALU.mult, ALU.add), [b_xp, b_small] + psb(ybk), [b_dy[tsl]])
                VO(lambda h, tt=tt: h.tensor_mul(dly2[:, tt, :], dly2[:, tt, :],
                                                 zsil[:, tt, :]), [b_z, b_dy[tt % 2]],
                   [b_dy[tt % 2]])
            for m in range(KT):
                sl = m % 2
                wop = wop_s[sl]
                DMA(wop, dr["m_outproj"][:, :, m, :], [b_wop[sl]])
                bk = 6 + sl
                for k in range(DIT):
                    MM(ps(bk, 512), wop[:, k, :], dly2[:, k, :],
                       k == 0, k == DIT - 1, [b_wop[sl], b_dy[k % 2]], psb(bk))
                dst = t_xi[:, m, 1 + c * 512: 1 + (c + 1) * 512]
                VO(lambda h, dst=dst, bk=bk: h.tensor_add(dst, dst, ps(bk, 512)),
                   [b_xi] + psb(bk), [b_xi])
        maybe_tap("m_afterssm", t_xi)
        G = view(t_xoa, 0, 16384, BF16, inner=1024)
        xn2 = view(t_ph, 0, 4096, BF16, inner=1024)
        w1b_s = [view(t_ph, 4096, 1024, BF16, inner=256),
                 view(t_ph, 5120, 1024, BF16, inner=256)]
        w3b_s = [view(t_ph, 6144, 1024, BF16, inner=256),
                 view(t_ph, 7168, 1024, BF16, inner=256)]
        w2b = view(t_ph, 8192, 2048, BF16, inner=128)
        ba2 = new_arena("swiglu")
        b_xn2 = B("xn2"); b_g2 = B("G2")
        b_w13 = [B("w13a"), B("w13b")]; b_w2m = B("w2m")
        for th in range(2):
            norm_apply(t_xi, lambda k, tc: xn2[:, k, (tc % 2) * 512:(tc % 2 + 1) * 512],
                       False, "rms", g_ap=small(O_N2, KT), tcs=[2 * th, 2 * th + 1],
                       dst_bufs=[b_xn2])
            for blk in range(16):
                sl = blk % 2
                w1b, w3b = w1b_s[sl], w3b_s[sl]
                DMA(w1b.rearrange("p k (m c) -> p k m c", c=128),
                    dr["m_w1"][:, :, blk * 2:(blk + 1) * 2, :], [b_w13[sl]])
                DMA(w3b.rearrange("p k (m c) -> p k m c", c=128),
                    dr["m_w3"][:, :, blk * 2:(blk + 1) * 2, :], [b_w13[sl]])
                for ft in range(2):
                    fg = blk * 2 + ft
                    fb = 4 * (ft % 2)
                    for k in range(KT):
                        for tq in range(2):
                            MM(ps(fb + tq, 512), w1b[:, k, ft * 128:(ft + 1) * 128],
                               xn2[:, k, tq * 512:(tq + 1) * 512], k == 0, k == KT - 1,
                               [b_w13[sl], b_xn2], psb(fb + tq))
                            MM(ps(fb + 2 + tq, 512), w3b[:, k, ft * 128:(ft + 1) * 128],
                               xn2[:, k, tq * 512:(tq + 1) * 512], k == 0, k == KT - 1,
                               [b_w13[sl], b_xn2], psb(fb + 2 + tq))
                    for tq in range(2):
                        ssl = tq
                        VO(lambda h, tq=tq, fb=fb, ssl=ssl: h.activation(
                            sc1_slot[ssl], ps(fb + tq, 512), AF.Silu),
                           psb(fb + tq), [b_sc1s[ssl]], eng="scalar")
                        VO(lambda h, fg=fg, tq=tq, fb=fb, ssl=ssl: h.tensor_mul(
                            G[:, fg, tq * 512:(tq + 1) * 512], sc1_slot[ssl],
                            ps(fb + 2 + tq, 512)), [b_sc1s[ssl]] + psb(fb + 2 + tq),
                           [b_g2])
            for m in range(KT):
                DMA(w2b, dr["m_w2"][:, :, m, :], [b_w2m])
                for tq in range(2):
                    bk = 4 + tq
                    for kf in range(32):
                        MM(ps(bk, 512), w2b[:, kf, :], G[:, kf, tq * 512:(tq + 1) * 512],
                           kf == 0, kf == 31, [b_w2m, b_g2], psb(bk))
                    tcg = th * 2 + tq
                    dst = t_xi[:, m, 1 + tcg * 512: 1 + (tcg + 1) * 512]
                    VO(lambda h, dst=dst, bk=bk: h.tensor_add(dst, dst, ps(bk, 512)),
                       [b_xi] + psb(bk), [b_xi])

    try:
        maybe_tap("input", xo)
        norm_apply(xo, lambda k, tc: t_xi[:, k, 1 + tc * 512: 1 + (tc + 1) * 512],
                   False, "ln", dst_bufs=[b_xi])
        maybe_tap("ln0", t_xi)
        spill_xo()
        for l in range(NLAYER):
            attn_sublayer(l)
            maybe_tap(f"inner_attn{l}", t_xi)
            ffn_sublayer(l)
            maybe_tap(f"inner_ffn{l}", t_xi)
        reload_xo(0.5)
        maybe_tap("after_ffn1", xo)
        norm_apply(xo, lambda k, tc: t_xi[:, k, 1 + tc * 512: 1 + (tc + 1) * 512],
                   False, "ln", dst_bufs=[b_xi])
        maybe_tap("m_in", t_xi)
        spill_xo()
        mamba_sublayer()
        reload_xo(1.0)
        maybe_tap("after_mamba", xo)
        norm_apply(xo, lambda k, tc: t_xi[:, k, 1 + tc * 512: 1 + (tc + 1) * 512],
                   False, "ln", dst_bufs=[b_xi])
        spill_xo()
        for l in range(NLAYER):
            attn_sublayer(l)
            ffn_sublayer(l)
        reload_xo(0.5)
        outs = []
        for k in range(KT):
            outs.append(P.op("sync", lambda h, k=k: h.dma_start(
                y[k, :, :], xo[:, k, 1:TP1]), reads=[b_xo], dma=True))
        P.finalize(outs)
    except _TapDone:
        pass
    return nc



def kernel(**inputs):
    H = host_prepare(inputs)
    x = np.ascontiguousarray(np.asarray(inputs["L0_attn_x"], dtype=np.float32))
    nc = build_program()
    in_maps = []
    for b in range(8):
        m = dict(H)
        xb = np.ascontiguousarray(x[b].T)
        m["x"] = np.ascontiguousarray(xb.reshape(KT, 128, T).transpose(1, 0, 2))
        in_maps.append(m)
    res = run_bass_kernel_spmd(nc, in_maps, core_ids=list(range(8)))
    out = np.empty((8, T, D), np.float32)
    for b in range(8):
        yb = np.asarray(res.results[b]["y"])
        out[b] = yb.transpose(2, 0, 1).reshape(T, D)
    return out
